# version 01
# baseline (speedup 1.0000x reference)
import math
from contextlib import ExitStack

import numpy as np
import concourse.bass as bass
import concourse.mybir as mybir
from concourse.bass_utils import run_bass_kernel_spmd

F32 = mybir.dt.float32
BF16 = mybir.dt.bfloat16
AF = mybir.ActivationFunctionType
ALU = mybir.AluOpType

NCORES = 8
B = 4
T = 4096
D = 1024
TS = T // NCORES
DC = 128
HD = 64
NHEAD = 16
HPC = 2
DR = 32
DI = 2048
DSL = DI // NCORES
NST = 16
DTR = 64
FF = 4096
EPS = 1e-5
QSCALE = 1.0 / math.sqrt(HD + DR)
ESHIFT = -12.0
NTD = D // 128
NTF = FF // 128
NTI = DI // 128
QC = 512
KT = T // 128


def _bar(nc):
    nc.all_engine_barrier()

def dma_load_tiles(eng, sbuf, W, dram2d, NT, sem):
    for i in range(NT):
        eng.dma_start(out=sbuf[:, i * W:(i + 1) * W],
                      in_=dram2d[i * 128:(i + 1) * 128, :]).then_inc(sem, 16)
    return 16 * NT


def dma_store_tiles(eng, dram2d, sbuf, W, NT, sem):
    for i in range(NT):
        eng.dma_start(out=dram2d[i * 128:(i + 1) * 128, :],
                      in_=sbuf[:, i * W:(i + 1) * W]).then_inc(sem, 16)
    return 16 * NT



def emit_ln(nc, sc, src, dst, W, gamma=None, beta=None, dst_is_f32=False):
    NT = len(src)
    xb, sq = sc["xb"], sc["sq"]
    mean, var = sc["mean"], sc["var"]
    arow, brow = sc["arow"], sc["brow"]
    hrow, lrow = sc["hrow"], sc["lrow"]
    lnf = sc["lnf"]
    onesb = sc["onesb"]
    ones1 = sc["ones1"]
    ps_sx, ps_sq, ps_ab = sc["ps_sx"], sc["ps_sq"], sc["ps_ab"]

    for i in range(NT):
        nc.vector.tensor_copy(xb[:, i * W:(i + 1) * W], src[i])
        nc.scalar.activation(sq[:, i * W:(i + 1) * W], src[i], AF.Square)
    _bar(nc)
    for i in range(NT):
        nc.tensor.matmul(ps_sx[:], onesb[:], xb[:, i * W:(i + 1) * W],
                         start=(i == 0), stop=(i == NT - 1))
    for i in range(NT):
        nc.tensor.matmul(ps_sq[:], onesb[:], sq[:, i * W:(i + 1) * W],
                         start=(i == 0), stop=(i == NT - 1))
    _bar(nc)
    inv = 1.0 / (NT * 128)
    nc.vector.tensor_scalar(mean[:], ps_sx[:], inv, None, ALU.mult)
    nc.vector.tensor_scalar(var[:], ps_sq[:], inv, None, ALU.mult)
    nc.vector.tensor_tensor(brow[:], mean[:], mean[:], ALU.mult)
    nc.vector.tensor_tensor(var[:], var[:], brow[:], ALU.subtract)
    nc.vector.tensor_scalar(var[:], var[:], EPS, None, ALU.add)
    _bar(nc)
    nc.scalar.activation(arow[:], var[:], AF.Sqrt)
    _bar(nc)
    nc.vector.reciprocal(arow[:], arow[:])
    _bar(nc)
    nc.vector.tensor_tensor(brow[:], mean[:], arow[:], ALU.mult)
    nc.vector.tensor_scalar(brow[:], brow[:], -1.0, None, ALU.mult)
    _bar(nc)
    nc.vector.tensor_copy(hrow[:], arow[:])
    nc.vector.tensor_tensor(lrow[:], arow[:], hrow[:], ALU.subtract)
    _bar(nc)
    nc.tensor.matmul(ps_ab[:, 0:W], ones1[:], hrow[:], start=True, stop=False)
    nc.tensor.matmul(ps_ab[:, 0:W], ones1[:], lrow[:], start=False, stop=True)
    _bar(nc)
    nc.vector.tensor_copy(hrow[:], brow[:])
    nc.vector.tensor_tensor(lrow[:], brow[:], hrow[:], ALU.subtract)
    _bar(nc)
    nc.tensor.matmul(ps_ab[:, W:2 * W], ones1[:], hrow[:], start=True, stop=False)
    nc.tensor.matmul(ps_ab[:, W:2 * W], ones1[:], lrow[:], start=False, stop=True)
    _bar(nc)
    for i in range(NT):
        nc.vector.tensor_tensor(lnf[:], src[i], ps_ab[:, 0:W], ALU.mult)
        if gamma is None:
            nc.vector.tensor_tensor(dst[i], lnf[:], ps_ab[:, W:2 * W], ALU.add)
        else:
            nc.vector.tensor_tensor(lnf[:], lnf[:], ps_ab[:, W:2 * W], ALU.add)
            nc.vector.tensor_scalar(lnf[:], lnf[:], gamma[i], None, ALU.mult)
            nc.vector.tensor_scalar(dst[i], lnf[:], beta[i], None, ALU.add)
    _bar(nc)


def emit_ffn(nc, sc, xln_tiles, resid_tiles, out_tiles, wr_t, w1_t, b1_t, w2_t, b2_t,
             ld_sem, ld_ctr, w1d, w2d, pref):
    ps_r = sc["ps_r"]
    ps_sum = sc["ps_sum"]
    ps_ab = sc["ps_ab"]
    ps_h = sc["ps_h"]
    expl = sc["expl"]
    exph, expll = sc["exph"], sc["expll"]
    ones8 = sc["ones8"]
    ones1 = sc["ones1"]
    srow = sc["srow"]
    hrow, lrow = sc["hrow"], sc["lrow"]
    p7rep = sc["p7rep"]
    hbuf = sc["hbuf"]
    ffny = sc["lnf"]

    for i in range(NTD):
        nc.tensor.matmul(ps_r[:], wr_t[:, i * 8:(i + 1) * 8], xln_tiles[i],
                         start=(i == 0), stop=(i == NTD - 1))
    _bar(nc)
    nc.scalar.activation(expl[:], ps_r[:], AF.Exp)
    _bar(nc)
    nc.vector.tensor_copy(exph[:], expl[:])
    nc.vector.tensor_tensor(expll[:], expl[:], exph[:], ALU.subtract)
    _bar(nc)
    nc.tensor.matmul(ps_sum[:], ones8[:], exph[:], start=True, stop=False)
    nc.tensor.matmul(ps_sum[:], ones8[:], expll[:], start=False, stop=True)
    _bar(nc)
    nc.vector.reciprocal(srow[:], ps_sum[:])
    _bar(nc)
    nc.vector.tensor_tensor(srow[:], expl[0:1, :], srow[:], ALU.mult)
    _bar(nc)
    nc.vector.tensor_copy(hrow[:], srow[:])
    nc.vector.tensor_tensor(lrow[:], srow[:], hrow[:], ALU.subtract)
    _bar(nc)
    nc.tensor.matmul(ps_ab[:, 0:TS], ones1[:], hrow[:], start=True, stop=False)
    nc.tensor.matmul(ps_ab[:, 0:TS], ones1[:], lrow[:], start=False, stop=True)
    _bar(nc)
    nc.vector.tensor_copy(p7rep[:], ps_ab[:, 0:TS])
    _bar(nc)

    for fot in range(NTF):
        buf = w1_t[fot % 2]
        ld_ctr[0] += dma_load_tiles(nc.gpsimd, buf, 128,
                                    w1d[:, fot * 128:(fot + 1) * 128], NTD, ld_sem)
        nc.tensor.wait_ge(ld_sem, ld_ctr[0])
        bank = fot % 2
        for i in range(NTD):
            nc.tensor.matmul(ps_h[:, bank * QC:(bank + 1) * QC],
                             buf[:, i * 128:(i + 1) * 128], xln_tiles[i],
                             start=(i == 0), stop=(i == NTD - 1))
        _bar(nc)
        nc.scalar.activation(hbuf[:, fot * TS:(fot + 1) * TS],
                             ps_h[:, bank * QC:(bank + 1) * QC],
                             AF.Gelu, bias=b1_t[fot])
        _bar(nc)

    for ot in range(NTD):
        buf = w2_t[ot % 2]
        ld_ctr[0] += dma_load_tiles(nc.gpsimd, buf, 128,
                                    w2d[:, ot * 128:(ot + 1) * 128], NTF, ld_sem)
        nc.tensor.wait_ge(ld_sem, ld_ctr[0])
        bank = ot % 2
        for i in range(NTF):
            nc.tensor.matmul(ps_h[:, bank * QC:(bank + 1) * QC],
                             buf[:, i * 128:(i + 1) * 128],
                             hbuf[:, i * TS:(i + 1) * TS],
                             start=(i == 0), stop=(i == NTF - 1))
        _bar(nc)
        nc.vector.scalar_tensor_tensor(ffny[:], ps_h[:, bank * QC:(bank + 1) * QC],
                                       b2_t[ot], p7rep[:], ALU.add, ALU.mult)
        nc.vector.tensor_tensor(out_tiles[ot], resid_tiles[ot], ffny[:], ALU.add)
        _bar(nc)


def ffn_scratch(nc, es):
    sc = {}
    sc["xb"] = es.enter_context(nc.sbuf_tensor("ln_xb", [128, NTD * TS], BF16))
    sc["sq"] = es.enter_context(nc.sbuf_tensor("ln_sq", [128, NTD * TS], BF16))
    sc["mean"] = es.enter_context(nc.sbuf_tensor("ln_mean", [1, TS], F32))
    sc["var"] = es.enter_context(nc.sbuf_tensor("ln_var", [1, TS], F32))
    sc["arow"] = es.enter_context(nc.sbuf_tensor("ln_arow", [1, TS], F32))
    sc["brow"] = es.enter_context(nc.sbuf_tensor("ln_brow", [1, TS], F32))
    sc["hrow"] = es.enter_context(nc.sbuf_tensor("ln_hrow", [1, TS], BF16))
    sc["lrow"] = es.enter_context(nc.sbuf_tensor("ln_lrow", [1, TS], BF16))
    sc["lnf"] = es.enter_context(nc.sbuf_tensor("ln_lnf", [128, TS], F32))
    sc["onesb"] = es.enter_context(nc.sbuf_tensor("c_onesb", [128, 1], BF16))
    sc["ones1"] = es.enter_context(nc.sbuf_tensor("c_ones1", [1, 128], BF16))
    sc["ones8"] = es.enter_context(nc.sbuf_tensor("c_ones8", [8, 1], BF16))
    sc["expl"] = es.enter_context(nc.sbuf_tensor("expl", [8, TS], F32))
    sc["exph"] = es.enter_context(nc.sbuf_tensor("exph", [8, TS], BF16))
    sc["expll"] = es.enter_context(nc.sbuf_tensor("expll", [8, TS], BF16))
    sc["srow"] = es.enter_context(nc.sbuf_tensor("srow", [1, TS], F32))
    sc["p7rep"] = es.enter_context(nc.sbuf_tensor("p7rep", [128, TS], F32))
    sc["hbuf"] = es.enter_context(nc.sbuf_tensor("hbuf", [128, NTF * TS], BF16))
    sc["ps_sx"] = es.enter_context(nc.psum_tensor("ps_sx", [1, TS], F32))
    sc["ps_sq"] = es.enter_context(nc.psum_tensor("ps_sq", [1, TS], F32))
    sc["ps_ab"] = es.enter_context(nc.psum_tensor("ps_ab", [128, 2 * TS], F32))
    sc["ps_r"] = es.enter_context(nc.psum_tensor("ps_r", [8, TS], F32))
    sc["ps_h"] = es.enter_context(nc.psum_tensor("ps_h", [128, 2 * QC], F32))
    sc["ps_sum"] = sc["ps_sx"]
    return sc


def load_const(nc, sc, onesb_d, ones1_d, ones8_d, sem, ctr):
    nc.gpsimd.dma_start(out=sc["onesb"][:], in_=onesb_d).then_inc(sem, 16)
    nc.gpsimd.dma_start(out=sc["ones1"][:], in_=ones1_d).then_inc(sem, 16)
    nc.gpsimd.dma_start(out=sc["ones8"][:], in_=ones8_d).then_inc(sem, 16)
    ctr[0] += 48


def build_l1():
    nc = bass.Bass("TRN2", target_bir_lowering=False)
    x_in = nc.dram_tensor("x", [B, D, TS], F32, kind="ExternalInput").ap()
    wkv = nc.dram_tensor("wkv", [D, DC], F32, kind="ExternalInput").ap()
    onesb_d = nc.dram_tensor("onesb", [128, 1], F32, kind="ExternalInput").ap()
    ones1_d = nc.dram_tensor("ones1", [1, 128], F32, kind="ExternalInput").ap()
    ones8_d = nc.dram_tensor("ones8", [8, 1], F32, kind="ExternalInput").ap()
    xln_out = nc.dram_tensor("xln1", [B, D, TS], BF16, kind="ExternalOutput").ap()
    ckv_out = nc.dram_tensor("ckv", [B, DC, TS], BF16, kind="ExternalOutput").ap()

    with ExitStack() as es:
        sc = ffn_scratch(nc, es)
        xt = es.enter_context(nc.sbuf_tensor("xt", [128, NTD * TS], F32))
        xln = es.enter_context(nc.sbuf_tensor("xln", [128, NTD * TS], BF16))
        wkvt = es.enter_context(nc.sbuf_tensor("wkvt", [128, NTD * DC], BF16))
        ckvb = es.enter_context(nc.sbuf_tensor("ckvb", [128, TS], BF16))
        ps_ckv = es.enter_context(nc.psum_tensor("ps_ckv", [128, TS], F32))
        s_ld = es.enter_context(nc.semaphore("s_ld"))
        s_g = es.enter_context(nc.semaphore("s_g"))
        s_st = es.enter_context(nc.semaphore("s_st"))
        ldc = [0]; gc = [0]; stc = [0]

        load_const(nc, sc, onesb_d, ones1_d, ones8_d, s_g, gc)
        gc[0] += dma_load_tiles(nc.gpsimd, wkvt, DC, wkv, NTD, s_g)
        nc.tensor.wait_ge(s_g, gc[0])
        nc.vector.wait_ge(s_g, gc[0])
        _bar(nc)
        for b in range(B):
            ldc[0] += dma_load_tiles(nc.sync, xt, TS, x_in[b], NTD, s_ld)
            nc.vector.wait_ge(s_ld, ldc[0])
            nc.scalar.wait_ge(s_ld, ldc[0])
            _bar(nc)
            src = [xt[:, i * TS:(i + 1) * TS] for i in range(NTD)]
            dst = [xln[:, i * TS:(i + 1) * TS] for i in range(NTD)]
            emit_ln(nc, sc, src, dst, TS)
            for i in range(NTD):
                nc.tensor.matmul(ps_ckv[:], wkvt[:, i * DC:(i + 1) * DC], dst[i],
                                 start=(i == 0), stop=(i == NTD - 1))
            _bar(nc)
            nc.vector.tensor_copy(ckvb[:], ps_ckv[:])
            _bar(nc)
            stc[0] += dma_store_tiles(nc.sync, xln_out[b], xln, TS, NTD, s_st)
            nc.sync.dma_start(out=ckv_out[b], in_=ckvb[:]).then_inc(s_st, 16)
            stc[0] += 16
            nc.sync.wait_ge(s_st, stc[0])
            _bar(nc)
    return nc


def build_l2():
    nc = bass.Bass("TRN2", target_bir_lowering=False)
    xlnf = nc.dram_tensor("xlnf", [B, D, T], BF16, kind="ExternalInput").ap()
    ckvf = nc.dram_tensor("ckvf", [B, DC, T], BF16, kind="ExternalInput").ap()
    wqh = nc.dram_tensor("wqh", [D, 128], F32, kind="ExternalInput").ap()
    wqr = nc.dram_tensor("wqr", [D, 64], F32, kind="ExternalInput").ap()
    wqr2 = nc.dram_tensor("wqr2", [D, 64], F32, kind="ExternalInput").ap()
    wkuh = nc.dram_tensor("wkuh", [DC, 128], F32, kind="ExternalInput").ap()
    wkr = nc.dram_tensor("wkr", [DC, 64], F32, kind="ExternalInput").ap()
    wkr2 = nc.dram_tensor("wkr2", [DC, 64], F32, kind="ExternalInput").ap()
    wvarr = nc.dram_tensor("wvarr", [DC, 128], F32, kind="ExternalInput").ap()
    cosrep = nc.dram_tensor("cosrep", [128, T], F32, kind="ExternalInput").ap()
    sinsgn = nc.dram_tensor("sinsgn", [128, T], F32, kind="ExternalInput").ap()
    masks = nc.dram_tensor("masks", [4, 128, QC], F32, kind="ExternalInput").ap()
    ident = nc.dram_tensor("ident", [128, 128], F32, kind="ExternalInput").ap()
    csts = nc.dram_tensor("csts", [128, 2], F32, kind="ExternalInput").ap()
    attn_out = nc.dram_tensor("attn", [B, 128, T], BF16, kind="ExternalOutput").ap()

    with ExitStack() as es:
        xch = es.enter_context(nc.sbuf_tensor("xch", [128, NTD * QC], BF16))
        ckvb = es.enter_context(nc.sbuf_tensor("ckvb", [128, T], BF16))
        wqht = es.enter_context(nc.sbuf_tensor("wqht", [128, NTD * 128], BF16))
        wqrt = es.enter_context(nc.sbuf_tensor("wqrt", [128, NTD * 64], BF16))
        wqr2t = es.enter_context(nc.sbuf_tensor("wqr2t", [128, NTD * 64], BF16))
        wkuht = es.enter_context(nc.sbuf_tensor("wkuht", [128, 128], BF16))
        wkrt = es.enter_context(nc.sbuf_tensor("wkrt", [128, 64], BF16))
        wkr2t = es.enter_context(nc.sbuf_tensor("wkr2t", [128, 64], BF16))
        wvat = es.enter_context(nc.sbuf_tensor("wvat", [128, 128], BF16))
        cosr = es.enter_context(nc.sbuf_tensor("cosr", [128, T], F32))
        sinr = es.enter_context(nc.sbuf_tensor("sinr", [128, T], F32))
        mask01 = es.enter_context(nc.sbuf_tensor("mask01", [128, 4 * QC], BF16))
        idt = es.enter_context(nc.sbuf_tensor("idt", [128, 128], BF16))
        cstt = es.enter_context(nc.sbuf_tensor("cstt", [128, 2], F32))
        qraw = es.enter_context(nc.sbuf_tensor("qraw", [128, T], BF16))
        kraw = es.enter_context(nc.sbuf_tensor("kraw", [128, T], BF16))
        rawa = es.enter_context(nc.sbuf_tensor("rawa", [128, T], F32))
        rawb = es.enter_context(nc.sbuf_tensor("rawb", [128, T], F32))
        vbuf = es.enter_context(nc.sbuf_tensor("vbuf", [128, KT * 130], BF16))
        qcat = es.enter_context(nc.sbuf_tensor("qcat", [128, 2 * T], BF16))
        kcat = es.enter_context(nc.sbuf_tensor("kcat", [128, 2 * T], BF16))
        pbuf = es.enter_context(nc.sbuf_tensor("pbuf", [128, KT * QC], BF16))
        atok = es.enter_context(nc.sbuf_tensor("atok", [128, 64], BF16))
        attnT = es.enter_context(nc.sbuf_tensor("attnT", [128, T], BF16))
        rcol = es.enter_context(nc.sbuf_tensor("rcol", [128, 1], F32))
        ps_a = es.enter_context(nc.psum_tensor("ps_a", [128, 2 * QC], F32))
        ps_s = es.enter_context(nc.psum_tensor("ps_s", [128, 4 * QC], F32))
        ps_av = es.enter_context(nc.psum_tensor("ps_av", [128, 65], F32))
        ps_tp = es.enter_context(nc.psum_tensor("ps_tp", [64, 128], BF16))
        s_ld = es.enter_context(nc.semaphore("s_ld"))
        s_g = es.enter_context(nc.semaphore("s_g"))
        s_st = es.enter_context(nc.semaphore("s_st"))
        ldc = [0]; gc = [0]; stc = [0]

        def gload(dst_ap, src_ap):
            nc.gpsimd.dma_start(out=dst_ap, in_=src_ap).then_inc(s_g, 16)
            gc[0] += 16

        gc[0] += dma_load_tiles(nc.gpsimd, wqht, 128, wqh, NTD, s_g)
        gc[0] += dma_load_tiles(nc.gpsimd, wqrt, 64, wqr, NTD, s_g)
        gc[0] += dma_load_tiles(nc.gpsimd, wqr2t, 64, wqr2, NTD, s_g)
        gload(wkuht[:], wkuh)
        gload(wkrt[:], wkr)
        gload(wkr2t[:], wkr2)
        gload(wvat[:], wvarr)
        gload(cosr[:], cosrep)
        gload(sinr[:], sinsgn)
        for i in range(4):
            gload(mask01[:, i * QC:(i + 1) * QC], masks[i])
        gload(idt[:], ident)
        gload(cstt[:], csts)
        nc.tensor.wait_ge(s_g, gc[0])
        nc.vector.wait_ge(s_g, gc[0])
        nc.vector.memset(vbuf[:], 1.0)
        _bar(nc)

        for b in range(B):
            nc.sync.dma_start(out=ckvb[:], in_=ckvf[b]).then_inc(s_ld, 16)
            ldc[0] += 16
            nc.tensor.wait_ge(s_ld, ldc[0])
            for ch in range(NTD):
                ldc[0] += dma_load_tiles(nc.sync, xch, QC,
                                         xlnf[b][:, ch * QC:(ch + 1) * QC], NTD, s_ld)
                nc.tensor.wait_ge(s_ld, ldc[0])
                cc = slice(ch * QC, (ch + 1) * QC)
                for i in range(NTD):
                    nc.tensor.matmul(ps_a[:, 0:QC], wqht[:, i * 128:(i + 1) * 128],
                                     xch[:, i * QC:(i + 1) * QC],
                                     start=(i == 0), stop=(i == NTD - 1))
                for i in range(NTD):
                    nc.tensor.matmul(ps_a[0:64, QC:2 * QC], wqrt[:, i * 64:(i + 1) * 64],
                                     xch[:, i * QC:(i + 1) * QC],
                                     start=(i == 0), stop=(i == NTD - 1))
                nc.tensor.matmul(ps_a[64:128, QC:2 * QC], wkrt[:], ckvb[:, cc],
                                 start=True, stop=True)
                _bar(nc)
                nc.vector.tensor_copy(qraw[:, cc], ps_a[:, 0:QC])
                nc.vector.tensor_copy(rawa[:, cc], ps_a[:, QC:2 * QC])
                _bar(nc)
                for i in range(NTD):
                    nc.tensor.matmul(ps_a[0:64, 0:QC], wqr2t[:, i * 64:(i + 1) * 64],
                                     xch[:, i * QC:(i + 1) * QC],
                                     start=(i == 0), stop=(i == NTD - 1))
                nc.tensor.matmul(ps_a[64:128, 0:QC], wkr2t[:], ckvb[:, cc],
                                 start=True, stop=True)
                nc.tensor.matmul(ps_a[:, QC:2 * QC], wkuht[:], ckvb[:, cc],
                                 start=True, stop=True)
                _bar(nc)
                nc.vector.tensor_copy(rawb[:, cc], ps_a[:, 0:QC])
                nc.vector.tensor_copy(kraw[:, cc], ps_a[:, QC:2 * QC])
                _bar(nc)
                for kk in range(4):
                    kt = ch * 4 + kk
                    nc.tensor.matmul(ps_a[:, kk * 128:(kk + 1) * 128],
                                     ckvb[:, kt * 128:(kt + 1) * 128], wvat[:],
                                     start=True, stop=True)
                _bar(nc)
                for kk in range(4):
                    kt = ch * 4 + kk
                    nc.vector.tensor_copy(vbuf[:, kt * 130:kt * 130 + 64],
                                          ps_a[:, kk * 128:kk * 128 + 64])
                    nc.vector.tensor_copy(vbuf[:, kt * 130 + 65:kt * 130 + 129],
                                          ps_a[:, kk * 128 + 64:kk * 128 + 128])
                _bar(nc)
            nc.vector.tensor_tensor(rawa[:], rawa[:], cosr[:], ALU.mult)
            nc.vector.tensor_tensor(rawb[:], rawb[:], sinr[:], ALU.mult)
            nc.vector.tensor_tensor(rawa[:], rawa[:], rawb[:], ALU.add)
            _bar(nc)
            for h in range(HPC):
                qc_t = qcat[:, h * T:(h + 1) * T]
                kc_t = kcat[:, h * T:(h + 1) * T]
                nc.vector.tensor_copy(qc_t[0:64, :], qraw[h * 64:(h + 1) * 64, :])
                nc.vector.tensor_copy(qc_t[64:128, :], rawa[0:64, :])
                nc.vector.tensor_copy(kc_t[0:64, :], kraw[h * 64:(h + 1) * 64, :])
                nc.vector.tensor_copy(kc_t[64:128, :], rawa[64:128, :])
            _bar(nc)
            for h in range(HPC):
                qc_t = qcat[:, h * T:(h + 1) * T]
                kc_t = kcat[:, h * T:(h + 1) * T]
                for qc in range(T // QC):
                    nkt = 4 * qc + 4
                    qq = slice(qc * QC, (qc + 1) * QC)
                    for g in range((nkt + 3) // 4):
                        for kk in range(4 * g, min(4 * g + 4, nkt)):
                            nc.tensor.matmul(
                                ps_s[:, (kk % 4) * QC:(kk % 4 + 1) * QC],
                                kc_t[:, kk * 128:(kk + 1) * 128], qc_t[:, qq],
                                start=True, stop=True)
                        _bar(nc)
                        for kk in range(4 * g, min(4 * g + 4, nkt)):
                            nc.scalar.activation(
                                pbuf[:, kk * QC:(kk + 1) * QC],
                                ps_s[:, (kk % 4) * QC:(kk % 4 + 1) * QC],
                                AF.Exp, scale=cstt[:, 0:1], bias=cstt[:, 1:2])
                        _bar(nc)
                    for i in range(4):
                        kk = 4 * qc + i
                        nc.vector.tensor_tensor(
                            pbuf[:, kk * QC:(kk + 1) * QC],
                            pbuf[:, kk * QC:(kk + 1) * QC],
                            mask01[:, i * QC:(i + 1) * QC], ALU.mult)
                    _bar(nc)
                    for sub in range(4):
                        sq = qc * 4 + sub
                        for kk in range(sq + 1):
                            nc.tensor.matmul(
                                ps_av[:],
                                pbuf[:, kk * QC + sub * 128: kk * QC + (sub + 1) * 128],
                                vbuf[:, kk * 130 + h * 65: kk * 130 + (h + 1) * 65],
                                start=(kk == 0), stop=(kk == sq))
                        _bar(nc)
                        nc.vector.reciprocal(rcol[:], ps_av[:, 64:65])
                        _bar(nc)
                        nc.vector.tensor_scalar(atok[:], ps_av[:, 0:64], rcol[:], None, ALU.mult)
                        _bar(nc)
                        nc.tensor.transpose(ps_tp[:], atok[:], idt[:])
                        _bar(nc)
                        nc.vector.tensor_copy(
                            attnT[h * 64:(h + 1) * 64, sq * 128:(sq + 1) * 128],
                            ps_tp[0:64, :])
                        _bar(nc)
            nc.sync.dma_start(out=attn_out[b], in_=attnT[:]).then_inc(s_st, 16)
            stc[0] += 16
            nc.sync.wait_ge(s_st, stc[0])
            _bar(nc)
    return nc


def build_l3():
    nc = bass.Bass("TRN2", target_bir_lowering=False)
    attn = nc.dram_tensor("attn", [B, D, TS], BF16, kind="ExternalInput").ap()
    x_in = nc.dram_tensor("x", [B, D, TS], F32, kind="ExternalInput").ap()
    wo = nc.dram_tensor("wo", [D, D], F32, kind="ExternalInput").ap()
    wrt = nc.dram_tensor("wrt", [D, 8], F32, kind="ExternalInput").ap()
    w1d = nc.dram_tensor("w1", [D, FF], F32, kind="ExternalInput").ap()
    b1d = nc.dram_tensor("b1", [128, NTF], F32, kind="ExternalInput").ap()
    w2d = nc.dram_tensor("w2", [FF, D], F32, kind="ExternalInput").ap()
    b2d = nc.dram_tensor("b2", [128, NTD], F32, kind="ExternalInput").ap()
    gmd = nc.dram_tensor("gamma", [128, NTD], F32, kind="ExternalInput").ap()
    btd = nc.dram_tensor("beta", [128, NTD], F32, kind="ExternalInput").ap()
    onesb_d = nc.dram_tensor("onesb", [128, 1], F32, kind="ExternalInput").ap()
    ones1_d = nc.dram_tensor("ones1", [1, 128], F32, kind="ExternalInput").ap()
    ones8_d = nc.dram_tensor("ones8", [8, 1], F32, kind="ExternalInput").ap()
    x2_out = nc.dram_tensor("x2", [B, D, TS], F32, kind="ExternalOutput").ap()
    xl3a_out = nc.dram_tensor("xl3a", [B, D, TS], F32, kind="ExternalOutput").ap()
    xln3_out = nc.dram_tensor("xln3", [B, D, TS], BF16, kind="ExternalOutput").ap()

    with ExitStack() as es:
        sc = ffn_scratch(nc, es)
        at = es.enter_context(nc.sbuf_tensor("at", [128, NTD * TS], BF16))
        xt = es.enter_context(nc.sbuf_tensor("xt", [128, NTD * TS], F32))
        x1 = es.enter_context(nc.sbuf_tensor("x1", [128, NTD * TS], F32))
        x2 = es.enter_context(nc.sbuf_tensor("sb_x2", [128, NTD * TS], F32))
        xln2 = es.enter_context(nc.sbuf_tensor("xln2", [128, NTD * TS], BF16))
        xl3a = es.enter_context(nc.sbuf_tensor("sb_xl3a", [128, NTD * TS], F32))
        xln3 = es.enter_context(nc.sbuf_tensor("sb_xln3", [128, NTD * TS], BF16))
        wot = es.enter_context(nc.sbuf_tensor("wot", [128, NTD * D], BF16))
        wrtt = es.enter_context(nc.sbuf_tensor("wrtt", [128, NTD * 8], BF16))
        w1t = [es.enter_context(nc.sbuf_tensor(f"w1t{i}", [128, NTD * 128], BF16)) for i in range(2)]
        w2t = [es.enter_context(nc.sbuf_tensor(f"w2t{i}", [128, NTF * 128], BF16)) for i in range(2)]
        b1t = es.enter_context(nc.sbuf_tensor("b1t", [128, NTF], F32))
        b2t = es.enter_context(nc.sbuf_tensor("b2t", [128, NTD], F32))
        gmt = es.enter_context(nc.sbuf_tensor("gmt", [128, NTD], F32))
        btt = es.enter_context(nc.sbuf_tensor("btt", [128, NTD], F32))
        s_ld = es.enter_context(nc.semaphore("s_ld"))
        s_g = es.enter_context(nc.semaphore("s_g"))
        s_st = es.enter_context(nc.semaphore("s_st"))
        ldc = [0]; gc = [0]; stc = [0]

        load_const(nc, sc, onesb_d, ones1_d, ones8_d, s_g, gc)
        gc[0] += dma_load_tiles(nc.gpsimd, wot, D, wo, NTD, s_g)
        gc[0] += dma_load_tiles(nc.gpsimd, wrtt, 8, wrt, NTD, s_g)
        nc.gpsimd.dma_start(out=b1t[:], in_=b1d).then_inc(s_g, 16)
        nc.gpsimd.dma_start(out=b2t[:], in_=b2d).then_inc(s_g, 16)
        nc.gpsimd.dma_start(out=gmt[:], in_=gmd).then_inc(s_g, 16)
        nc.gpsimd.dma_start(out=btt[:], in_=btd).then_inc(s_g, 16)
        gc[0] += 64
        nc.tensor.wait_ge(s_g, gc[0])
        nc.vector.wait_ge(s_g, gc[0])
        _bar(nc)

        for b in range(B):
            ldc[0] += dma_load_tiles(nc.sync, at, TS, attn[b], NTD, s_ld)
            ldc[0] += dma_load_tiles(nc.sync, xt, TS, x_in[b], NTD, s_ld)
            nc.tensor.wait_ge(s_ld, ldc[0])
            nc.vector.wait_ge(s_ld, ldc[0])
            _bar(nc)
            for ot in range(NTD):
                bank = ot % 2
                for k in range(NTD):
                    nc.tensor.matmul(sc["ps_h"][:, bank * QC:(bank + 1) * QC],
                                     wot[:, k * D + ot * 128: k * D + (ot + 1) * 128],
                                     at[:, k * TS:(k + 1) * TS],
                                     start=(k == 0), stop=(k == NTD - 1))
                _bar(nc)
                nc.vector.tensor_tensor(x1[:, ot * TS:(ot + 1) * TS],
                                        xt[:, ot * TS:(ot + 1) * TS],
                                        sc["ps_h"][:, bank * QC:(bank + 1) * QC], ALU.add)
                _bar(nc)
            x1_tiles = [x1[:, i * TS:(i + 1) * TS] for i in range(NTD)]
            xln2_tiles = [xln2[:, i * TS:(i + 1) * TS] for i in range(NTD)]
            emit_ln(nc, sc, x1_tiles, xln2_tiles, TS)
            x2_tiles = [x2[:, i * TS:(i + 1) * TS] for i in range(NTD)]
            emit_ffn(nc, sc, xln2_tiles, x1_tiles, x2_tiles,
                     wrtt, w1t, [b1t[:, i:i + 1] for i in range(NTF)],
                     w2t, [b2t[:, i:i + 1] for i in range(NTD)],
                     s_ld, ldc, w1d, w2d, "f1")
            xl3a_tiles = [xl3a[:, i * TS:(i + 1) * TS] for i in range(NTD)]
            emit_ln(nc, sc, x2_tiles, xl3a_tiles, TS, dst_is_f32=True)
            xln3_tiles = [xln3[:, i * TS:(i + 1) * TS] for i in range(NTD)]
            emit_ln(nc, sc, xl3a_tiles, xln3_tiles, TS,
                    gamma=[gmt[:, i:i + 1] for i in range(NTD)],
                    beta=[btt[:, i:i + 1] for i in range(NTD)])
            stc[0] += dma_store_tiles(nc.sync, x2_out[b], x2, TS, NTD, s_st)
            stc[0] += dma_store_tiles(nc.sync, xl3a_out[b], xl3a, TS, NTD, s_st)
            stc[0] += dma_store_tiles(nc.sync, xln3_out[b], xln3, TS, NTD, s_st)
            nc.sync.wait_ge(s_st, stc[0])
            _bar(nc)
    return nc


def build_l4():
    nc = bass.Bass("TRN2", target_bir_lowering=False)
    xln3f = nc.dram_tensor("xln3f", [B, D, T], BF16, kind="ExternalInput").ap()
    winx = nc.dram_tensor("winx", [D, DSL], F32, kind="ExternalInput").ap()
    winz = nc.dram_tensor("winz", [D, DSL], F32, kind="ExternalInput").ap()
    cvd = nc.dram_tensor("cvd", [2, 4, 128, 128], F32, kind="ExternalInput").ap()
    bcv = nc.dram_tensor("bcv", [128, 2], F32, kind="ExternalInput").ap()
    wxa = nc.dram_tensor("wxa", [DSL, 128], F32, kind="ExternalInput").ap()
    u_out = nc.dram_tensor("u", [B, DSL, T], F32, kind="ExternalOutput").ap()
    z_out = nc.dram_tensor("z", [B, DSL, T], F32, kind="ExternalOutput").ap()
    xp_out = nc.dram_tensor("xdblp", [B, 128, T], F32, kind="ExternalOutput").ap()

    with ExitStack() as es:
        xch = es.enter_context(nc.sbuf_tensor("xch", [128, NTD * QC], BF16))
        winxt = es.enter_context(nc.sbuf_tensor("winxt", [128, NTD * DSL], BF16))
        winzt = es.enter_context(nc.sbuf_tensor("winzt", [128, NTD * DSL], BF16))
        cvdt = es.enter_context(nc.sbuf_tensor("cvdt", [128, 8 * 128], BF16))
        bcvt = es.enter_context(nc.sbuf_tensor("bcvt", [128, 2], F32))
        wxat = es.enter_context(nc.sbuf_tensor("wxat", [128, 2 * 128], BF16))
        xib = es.enter_context(nc.sbuf_tensor("xib", [128, 2 * (T + 3)], BF16))
        ub = es.enter_context(nc.sbuf_tensor("ub", [128, 2 * T], F32))
        ubb = es.enter_context(nc.sbuf_tensor("ubb", [128, 2 * T], BF16))
        zb = es.enter_context(nc.sbuf_tensor("zb", [128, 2 * T], F32))
        xpb = es.enter_context(nc.sbuf_tensor("xpb", [128, T], F32))
        ps4 = es.enter_context(nc.psum_tensor("ps4", [128, 4 * QC], F32))
        ps_c = es.enter_context(nc.psum_tensor("ps_c", [128, 2 * QC], F32))
        ps_x = es.enter_context(nc.psum_tensor("ps_x", [128, QC], F32))
        s_ld = es.enter_context(nc.semaphore("s_ld"))
        s_g = es.enter_context(nc.semaphore("s_g"))
        s_st = es.enter_context(nc.semaphore("s_st"))
        ldc = [0]; gc = [0]; stc = [0]

        gc[0] += dma_load_tiles(nc.gpsimd, winxt, DSL, winx, NTD, s_g)
        gc[0] += dma_load_tiles(nc.gpsimd, winzt, DSL, winz, NTD, s_g)
        for tl in range(2):
            for tap in range(4):
                nc.gpsimd.dma_start(out=cvdt[:, (tl * 4 + tap) * 128:(tl * 4 + tap + 1) * 128],
                                    in_=cvd[tl, tap]).then_inc(s_g, 16)
                gc[0] += 16
        nc.gpsimd.dma_start(out=bcvt[:], in_=bcv).then_inc(s_g, 16)
        gc[0] += 16
        gc[0] += dma_load_tiles(nc.gpsimd, wxat, 128, wxa, 2, s_g)
        nc.tensor.wait_ge(s_g, gc[0])
        nc.vector.wait_ge(s_g, gc[0])
        _bar(nc)

        for b in range(B):
            for tl in range(2):
                base = tl * (T + 3)
                nc.vector.memset(xib[:, base:base + 3], 0.0)
            _bar(nc)
            for ch in range(NTD):
                ldc[0] += dma_load_tiles(nc.sync, xch, QC,
                                         xln3f[b][:, ch * QC:(ch + 1) * QC], NTD, s_ld)
                nc.tensor.wait_ge(s_ld, ldc[0])
                for tl in range(2):
                    for i in range(NTD):
                        nc.tensor.matmul(ps4[:, tl * QC:(tl + 1) * QC],
                                         winxt[:, (i * 2 + tl) * 128:(i * 2 + tl + 1) * 128],
                                         xch[:, i * QC:(i + 1) * QC],
                                         start=(i == 0), stop=(i == NTD - 1))
                    for i in range(NTD):
                        nc.tensor.matmul(ps4[:, (2 + tl) * QC:(3 + tl) * QC],
                                         winzt[:, (i * 2 + tl) * 128:(i * 2 + tl + 1) * 128],
                                         xch[:, i * QC:(i + 1) * QC],
                                         start=(i == 0), stop=(i == NTD - 1))
                _bar(nc)
                for tl in range(2):
                    nc.vector.tensor_copy(
                        xib[:, tl * (T + 3) + 3 + ch * QC: tl * (T + 3) + 3 + (ch + 1) * QC],
                        ps4[:, tl * QC:(tl + 1) * QC])
                    nc.vector.tensor_copy(zb[:, tl * T + ch * QC: tl * T + (ch + 1) * QC],
                                          ps4[:, (2 + tl) * QC:(3 + tl) * QC])
                _bar(nc)
            for tl in range(2):
                for ch in range(NTD):
                    for tap in range(4):
                        nc.tensor.matmul(
                            ps_c[:, (ch % 2) * QC:(ch % 2 + 1) * QC],
                            cvdt[:, (tl * 4 + tap) * 128:(tl * 4 + tap + 1) * 128],
                            xib[:, tl * (T + 3) + ch * QC + tap: tl * (T + 3) + ch * QC + tap + QC],
                            start=(tap == 0), stop=(tap == 3))
                    _bar(nc)
                    nc.scalar.activation(ub[:, tl * T + ch * QC: tl * T + (ch + 1) * QC],
                                         ps_c[:, (ch % 2) * QC:(ch % 2 + 1) * QC],
                                         AF.Silu, bias=bcvt[:, tl:tl + 1])
                    _bar(nc)
            nc.vector.tensor_copy(ubb[:, 0:T], ub[:, 0:T])
            nc.vector.tensor_copy(ubb[:, T:2 * T], ub[:, T:2 * T])
            _bar(nc)
            for ch in range(NTD):
                for tl in range(2):
                    nc.tensor.matmul(ps_x[:], wxat[:, tl * 128:(tl + 1) * 128],
                                     ubb[:, tl * T + ch * QC: tl * T + (ch + 1) * QC],
                                     start=(tl == 0), stop=(tl == 1))
                _bar(nc)
                nc.vector.tensor_copy(xpb[:, ch * QC:(ch + 1) * QC], ps_x[:])
                _bar(nc)
            stc[0] += dma_store_tiles(nc.sync, u_out[b], ub, T, 2, s_st)
            stc[0] += dma_store_tiles(nc.sync, z_out[b], zb, T, 2, s_st)
            nc.sync.dma_start(out=xp_out[b], in_=xpb[:]).then_inc(s_st, 16)
            stc[0] += 16
            nc.sync.wait_ge(s_st, stc[0])
            _bar(nc)
    return nc


def build_l5():
    nc = bass.Bass("TRN2", target_bir_lowering=False)
    xparts = nc.dram_tensor("xparts", [NCORES, B, 128, T], F32, kind="ExternalInput").ap()
    u_in = nc.dram_tensor("u", [B, DSL, T], F32, kind="ExternalInput").ap()
    alog = nc.dram_tensor("alog", [128, 32], F32, kind="ExternalInput").ap()
    wdt = nc.dram_tensor("wdt", [2, DTR, 128], F32, kind="ExternalInput").ap()
    bdt = nc.dram_tensor("bdt", [128, 2], F32, kind="ExternalInput").ap()
    dsk = nc.dram_tensor("dsk", [128, 2], F32, kind="ExternalInput").ap()
    ones1_d = nc.dram_tensor("ones1", [1, 128], F32, kind="ExternalInput").ap()
    y_out = nc.dram_tensor("y", [B, DSL, T], BF16, kind="ExternalOutput").ap()
    bcd = nc.dram_tensor("bcd", [4, 16, T], BF16)

    TC = 512
    NCH = T // TC

    with ExitStack() as es:
        xdbl = es.enter_context(nc.sbuf_tensor("xdbl", [128, T], F32))
        dtrb = es.enter_context(nc.sbuf_tensor("dtrb", [64, T], BF16))
        s1 = es.enter_context(nc.sbuf_tensor("s1", [16, T], BF16))
        s2 = es.enter_context(nc.sbuf_tensor("s2", [16, T], BF16))
        cstage = es.enter_context(nc.sbuf_tensor("c0", [16, T], F32))
        bcfl = es.enter_context(nc.sbuf_tensor("bcfl", [1, 16 * TC], BF16))
        ub = es.enter_context(nc.sbuf_tensor("ub", [128, 2 * T], F32))
        dtb = es.enter_context(nc.sbuf_tensor("dtb", [128, 2 * T], F32))
        dtus = es.enter_context(nc.sbuf_tensor("dtus", [128, 2 * TC], F32))
        asb = es.enter_context(nc.sbuf_tensor("asb", [128, 32], F32))
        carry = es.enter_context(nc.sbuf_tensor("carry", [128, 32], F32))
        tmp4 = es.enter_context(nc.sbuf_tensor("tmp4", [128, 4], F32))
        da = es.enter_context(nc.sbuf_tensor("da", [128, 4 * TC], F32))
        dbu = es.enter_context(nc.sbuf_tensor("dbu", [128, 4 * TC], F32))
        hb = es.enter_context(nc.sbuf_tensor("hb", [128, 4 * TC], F32))
        cp = es.enter_context(nc.sbuf_tensor("cp", [128, 4 * TC], F32))
        yb = es.enter_context(nc.sbuf_tensor("yb", [128, 2 * T], F32))
        onesw = es.enter_context(nc.sbuf_tensor("onesw", [1, 128], BF16))
        wdtt = es.enter_context(nc.sbuf_tensor("wdtt", [64, 2 * 128], BF16))
        bdtt = es.enter_context(nc.sbuf_tensor("bdtt", [128, 2], F32))
        dskt = es.enter_context(nc.sbuf_tensor("dskt", [128, 2], F32))
        ps_bc = es.enter_context(nc.psum_tensor("ps_bc", [128, 4 * TC], F32))
        ps_dt = es.enter_context(nc.psum_tensor("ps_dt", [128, 2 * QC], F32))
        s_ld = es.enter_context(nc.semaphore("s_ld"))
        s_g = es.enter_context(nc.semaphore("s_g"))
        s_b = es.enter_context(nc.semaphore("s_b"))
        s_st = es.enter_context(nc.semaphore("s_st"))
        ldc = [0]; gc = [0]; bc = [0]; stc = [0]

        nc.gpsimd.dma_start(out=asb[:], in_=alog).then_inc(s_g, 16)
        for tl in range(2):
            nc.gpsimd.dma_start(out=wdtt[:, tl * 128:(tl + 1) * 128], in_=wdt[tl]).then_inc(s_g, 16)
        nc.gpsimd.dma_start(out=bdtt[:], in_=bdt).then_inc(s_g, 16)
        nc.gpsimd.dma_start(out=dskt[:], in_=dsk).then_inc(s_g, 16)
        nc.gpsimd.dma_start(out=onesw[:], in_=ones1_d).then_inc(s_g, 16)
        gc[0] += 96
        nc.scalar.wait_ge(s_g, gc[0])
        _bar(nc)
        nc.scalar.activation(asb[:], asb[:], AF.Exp)
        _bar(nc)
        nc.vector.tensor_scalar(asb[:], asb[:], -1.0, None, ALU.mult)
        _bar(nc)

        for b in range(B):
            nc.sync.dma_start(out=xdbl[:], in_=xparts[0, b]).then_inc(s_ld, 16)
            ldc[0] += 16
            for r in range(1, NCORES):
                nc.sync.dma_start(out=ub[:, 0:T], in_=xparts[r, b]).then_inc(s_ld, 16)
                ldc[0] += 16
                nc.vector.wait_ge(s_ld, ldc[0])
                nc.vector.tensor_tensor(xdbl[:], xdbl[:], ub[:, 0:T], ALU.add)
                _bar(nc)
            nc.vector.tensor_copy(s1[:], xdbl[0:16, :])
            nc.vector.tensor_copy(dtrb[:], xdbl[64:128, :])
            _bar(nc)
            nc.vector.tensor_tensor(s2[:], xdbl[0:16, :], s1[:], ALU.subtract)
            _bar(nc)
            nc.sync.dma_start(out=bcd[0], in_=s1[:]).then_inc(s_b, 16)
            nc.sync.dma_start(out=bcd[1], in_=s2[:]).then_inc(s_b, 16)
            bc[0] += 32
            nc.sync.wait_ge(s_b, bc[0])
            _bar(nc)
            nc.vector.tensor_copy(cstage[:], xdbl[32:48, :])
            _bar(nc)
            nc.vector.tensor_copy(s1[:], cstage[:])
            _bar(nc)
            nc.vector.tensor_tensor(s2[:], cstage[:], s1[:], ALU.subtract)
            _bar(nc)
            nc.sync.dma_start(out=bcd[2], in_=s1[:]).then_inc(s_b, 16)
            nc.sync.dma_start(out=bcd[3], in_=s2[:]).then_inc(s_b, 16)
            bc[0] += 32
            nc.sync.wait_ge(s_b, bc[0])
            ldc[0] += dma_load_tiles(nc.sync, ub, T, u_in[b], 2, s_ld)
            for tl in range(2):
                for ch2 in range(T // QC):
                    nc.tensor.matmul(ps_dt[:, (ch2 % 2) * QC:(ch2 % 2 + 1) * QC],
                                     wdtt[:, tl * 128:(tl + 1) * 128],
                                     dtrb[:, ch2 * QC:(ch2 + 1) * QC],
                                     start=True, stop=True)
                    _bar(nc)
                    nc.scalar.activation(da[:, 0:QC], ps_dt[:, (ch2 % 2) * QC:(ch2 % 2 + 1) * QC],
                                         AF.Exp, bias=bdtt[:, tl:tl + 1])
                    _bar(nc)
                    nc.scalar.activation(dtb[:, tl * T + ch2 * QC: tl * T + (ch2 + 1) * QC],
                                         da[:, 0:QC], AF.Ln, bias=1.0)
                    _bar(nc)
            nc.vector.wait_ge(s_ld, ldc[0])
            nc.vector.memset(carry[:], 0.0)
            _bar(nc)
            for ch in range(NCH):
                cc = slice(ch * TC, (ch + 1) * TC)
                for tl in range(2):
                    nc.vector.tensor_tensor(dtus[:, tl * TC:(tl + 1) * TC],
                                            dtb[:, tl * T:][:, cc], ub[:, tl * T:][:, cc], ALU.mult)
                _bar(nc)
                for nh in range(4):
                    nc.sync.dma_start(
                        out=bcfl[:],
                        in_=bcd.ap()[:, nh * 4:(nh + 1) * 4, ch * TC:(ch + 1) * TC],
                    ).then_inc(s_b, 16)
                    bc[0] += 16
                    nc.tensor.wait_ge(s_b, bc[0])
                    for tl in range(2):
                        for k in range(4):
                            nc.vector.tensor_scalar(
                                da[:, k * TC:(k + 1) * TC], dtb[:, tl * T:][:, cc],
                                asb[:, tl * 16 + nh * 4 + k: tl * 16 + nh * 4 + k + 1],
                                None, ALU.mult)
                        _bar(nc)
                        nc.scalar.activation(da[:], da[:], AF.Exp)
                        for k in range(4):
                            nc.tensor.matmul(ps_bc[:, k * TC:(k + 1) * TC], onesw[:],
                                             bcfl[0:1, (0 * 4 + k) * TC:(0 * 4 + k + 1) * TC],
                                             start=True, stop=False)
                            nc.tensor.matmul(ps_bc[:, k * TC:(k + 1) * TC], onesw[:],
                                             bcfl[0:1, (1 * 4 + k) * TC:(1 * 4 + k + 1) * TC],
                                             start=False, stop=True)
                        _bar(nc)
                        for k in range(4):
                            nc.vector.tensor_tensor(dbu[:, k * TC:(k + 1) * TC],
                                                    dtus[:, tl * TC:(tl + 1) * TC],
                                                    ps_bc[:, k * TC:(k + 1) * TC], ALU.mult)
                        c0 = tl * 16 + nh * 4
                        _bar(nc)
                        for k in range(4):
                            nc.vector.scalar_tensor_tensor(
                                dbu[:, k * TC:k * TC + 1], da[:, k * TC:k * TC + 1],
                                carry[:, c0 + k:c0 + k + 1], dbu[:, k * TC:k * TC + 1],
                                ALU.mult, ALU.add)
                        _bar(nc)
                        for k in range(4):
                            nc.vector.memset(da[:, k * TC:k * TC + 1], 0.0)
                        _bar(nc)
                        nc.vector.tensor_tensor_scan(hb[:], da[:], dbu[:], 0.0,
                                                     ALU.mult, ALU.add)
                        _bar(nc)
                        for k in range(4):
                            nc.vector.tensor_copy(carry[:, c0 + k:c0 + k + 1],
                                                  hb[:, k * TC + TC - 1:k * TC + TC])
                        _bar(nc)
                        for k in range(4):
                            nc.tensor.matmul(ps_bc[:, k * TC:(k + 1) * TC], onesw[:],
                                             bcfl[0:1, (2 * 4 + k) * TC:(2 * 4 + k + 1) * TC],
                                             start=True, stop=False)
                            nc.tensor.matmul(ps_bc[:, k * TC:(k + 1) * TC], onesw[:],
                                             bcfl[0:1, (3 * 4 + k) * TC:(3 * 4 + k + 1) * TC],
                                             start=False, stop=True)
                        _bar(nc)
                        for k in range(4):
                            nc.vector.tensor_tensor(cp[:, k * TC:(k + 1) * TC],
                                                    hb[:, k * TC:(k + 1) * TC],
                                                    ps_bc[:, k * TC:(k + 1) * TC], ALU.mult)
                        nc.vector.tensor_tensor(cp[:, 0:TC], cp[:, 0:TC], cp[:, TC:2 * TC], ALU.add)
                        nc.vector.tensor_tensor(cp[:, 2 * TC:3 * TC], cp[:, 2 * TC:3 * TC],
                                                cp[:, 3 * TC:4 * TC], ALU.add)
                        nc.vector.tensor_tensor(cp[:, 0:TC], cp[:, 0:TC], cp[:, 2 * TC:3 * TC], ALU.add)
                        dst = yb[:, tl * T:][:, cc]
                        if nh == 0:
                            nc.vector.tensor_copy(dst, cp[:, 0:TC])
                        else:
                            nc.vector.tensor_tensor(dst, dst, cp[:, 0:TC], ALU.add)
                        _bar(nc)
            for tl in range(2):
                nc.vector.tensor_scalar(ub[:, tl * T:(tl + 1) * T], ub[:, tl * T:(tl + 1) * T],
                                        dskt[:, tl:tl + 1], None, ALU.mult)
                nc.vector.tensor_tensor(yb[:, tl * T:(tl + 1) * T], yb[:, tl * T:(tl + 1) * T],
                                        ub[:, tl * T:(tl + 1) * T], ALU.add)
            _bar(nc)
            stc[0] += dma_store_tiles(nc.gpsimd, y_out[b], yb, T, 2, s_st)
            nc.sync.wait_ge(s_st, stc[0])
            nc.gpsimd.wait_ge(s_st, stc[0])
            _bar(nc)
    return nc


def build_l6():
    nc = bass.Bass("TRN2", target_bir_lowering=False)
    yf = nc.dram_tensor("yf", [B, DI, TS], BF16, kind="ExternalInput").ap()
    zf = nc.dram_tensor("zf", [B, DI, TS], F32, kind="ExternalInput").ap()
    x2_in = nc.dram_tensor("x2", [B, D, TS], F32, kind="ExternalInput").ap()
    xl3a_in = nc.dram_tensor("xl3a", [B, D, TS], F32, kind="ExternalInput").ap()
    wout = nc.dram_tensor("wout", [DI, D], F32, kind="ExternalInput").ap()
    wrt = nc.dram_tensor("wrt", [D, 8], F32, kind="ExternalInput").ap()
    w1d = nc.dram_tensor("w1", [D, FF], F32, kind="ExternalInput").ap()
    b1d = nc.dram_tensor("b1", [128, NTF], F32, kind="ExternalInput").ap()
    w2d = nc.dram_tensor("w2", [FF, D], F32, kind="ExternalInput").ap()
    b2d = nc.dram_tensor("b2", [128, NTD], F32, kind="ExternalInput").ap()
    onesb_d = nc.dram_tensor("onesb", [128, 1], F32, kind="ExternalInput").ap()
    ones1_d = nc.dram_tensor("ones1", [1, 128], F32, kind="ExternalInput").ap()
    ones8_d = nc.dram_tensor("ones8", [8, 1], F32, kind="ExternalInput").ap()
    o_out = nc.dram_tensor("o", [B, D, TS], F32, kind="ExternalOutput").ap()

    with ExitStack() as es:
        sc = ffn_scratch(nc, es)
        yt = es.enter_context(nc.sbuf_tensor("yt", [128, NTI * TS], BF16))
        zt = es.enter_context(nc.sbuf_tensor("zt", [128, 2 * TS], F32))
        l3at = es.enter_context(nc.sbuf_tensor("l3at", [128, 2 * TS], F32))
        ym = es.enter_context(nc.sbuf_tensor("ym", [128, NTI * TS], BF16))
        x2t = es.enter_context(nc.sbuf_tensor("x2t", [128, NTD * TS], F32))
        x3 = es.enter_context(nc.sbuf_tensor("x3", [128, NTD * TS], F32))
        xln4 = es.enter_context(nc.sbuf_tensor("xln4", [128, NTD * TS], BF16))
        woutt = es.enter_context(nc.sbuf_tensor("woutt", [128, NTI * D], BF16))
        wrtt = es.enter_context(nc.sbuf_tensor("wrtt", [128, NTD * 8], BF16))
        w1t = [es.enter_context(nc.sbuf_tensor(f"w1t{i}", [128, NTD * 128], BF16)) for i in range(2)]
        w2t = [es.enter_context(nc.sbuf_tensor(f"w2t{i}", [128, NTF * 128], BF16)) for i in range(2)]
        b1t = es.enter_context(nc.sbuf_tensor("b1t", [128, NTF], F32))
        b2t = es.enter_context(nc.sbuf_tensor("b2t", [128, NTD], F32))
        s_ld = es.enter_context(nc.semaphore("s_ld"))
        s_g = es.enter_context(nc.semaphore("s_g"))
        s_st = es.enter_context(nc.semaphore("s_st"))
        ldc = [0]; gc = [0]; stc = [0]

        load_const(nc, sc, onesb_d, ones1_d, ones8_d, s_g, gc)
        gc[0] += dma_load_tiles(nc.gpsimd, woutt, D, wout, NTI, s_g)
        gc[0] += dma_load_tiles(nc.gpsimd, wrtt, 8, wrt, NTD, s_g)
        nc.gpsimd.dma_start(out=b1t[:], in_=b1d).then_inc(s_g, 16)
        nc.gpsimd.dma_start(out=b2t[:], in_=b2d).then_inc(s_g, 16)
        gc[0] += 32
        nc.tensor.wait_ge(s_g, gc[0])
        nc.vector.wait_ge(s_g, gc[0])
        _bar(nc)

        for b in range(B):
            ldc[0] += dma_load_tiles(nc.sync, yt, TS, yf[b], NTI, s_ld)
            ldc[0] += dma_load_tiles(nc.sync, x2t, TS, x2_in[b], NTD, s_ld)
            nc.scalar.wait_ge(s_ld, ldc[0])
            nc.vector.wait_ge(s_ld, ldc[0])
            _bar(nc)
            for i in range(NTI):
                zslot = zt[:, (i % 2) * TS:(i % 2 + 1) * TS]
                nc.sync.dma_start(out=zslot, in_=zf[b][i * 128:(i + 1) * 128, :]).then_inc(s_ld, 16)
                ldc[0] += 16
                nc.scalar.wait_ge(s_ld, ldc[0])
                nc.scalar.activation(zslot, zslot, AF.Silu)
                _bar(nc)
                nc.vector.tensor_tensor(ym[:, i * TS:(i + 1) * TS], yt[:, i * TS:(i + 1) * TS],
                                        zslot, ALU.mult)
                _bar(nc)
            for ot in range(NTD):
                bank = ot % 2
                slot = l3at[:, bank * TS:(bank + 1) * TS]
                nc.sync.dma_start(out=slot, in_=xl3a_in[b][ot * 128:(ot + 1) * 128, :]).then_inc(s_ld, 16)
                ldc[0] += 16
                for k in range(NTI):
                    nc.tensor.matmul(sc["ps_h"][:, bank * QC:(bank + 1) * QC],
                                     woutt[:, k * D + ot * 128: k * D + (ot + 1) * 128],
                                     ym[:, k * TS:(k + 1) * TS],
                                     start=(k == 0), stop=(k == NTI - 1))
                nc.vector.wait_ge(s_ld, ldc[0])
                _bar(nc)
                nc.vector.tensor_tensor(x3[:, ot * TS:(ot + 1) * TS],
                                        x2t[:, ot * TS:(ot + 1) * TS],
                                        sc["ps_h"][:, bank * QC:(bank + 1) * QC], ALU.add)
                _bar(nc)
                nc.vector.tensor_tensor(x3[:, ot * TS:(ot + 1) * TS],
                                        x3[:, ot * TS:(ot + 1) * TS], slot, ALU.add)
                _bar(nc)
            x3_tiles = [x3[:, i * TS:(i + 1) * TS] for i in range(NTD)]
            xln4_tiles = [xln4[:, i * TS:(i + 1) * TS] for i in range(NTD)]
            emit_ln(nc, sc, x3_tiles, xln4_tiles, TS)
            o_tiles = [x2t[:, i * TS:(i + 1) * TS] for i in range(NTD)]
            emit_ffn(nc, sc, xln4_tiles, x3_tiles, o_tiles,
                     wrtt, w1t, [b1t[:, i:i + 1] for i in range(NTF)],
                     w2t, [b2t[:, i:i + 1] for i in range(NTD)],
                     s_ld, ldc, w1d, w2d, "f2")
            stc[0] += dma_store_tiles(nc.sync, o_out[b], x2t, TS, NTD, s_st)
            nc.sync.wait_ge(s_st, stc[0])
            _bar(nc)
    return nc


def _run(nc, in_maps):
    return run_bass_kernel_spmd(nc, in_maps, core_ids=list(range(NCORES)))


def kernel(**inp):
    f32 = np.float32
    xT = np.ascontiguousarray(np.transpose(np.asarray(inp["L0_attn_a_x"], f32), (0, 2, 1)))

    onesb = np.ones((128, 1), f32)
    ones1 = np.ones((1, 128), f32)
    ones8 = np.ones((8, 1), f32)

    wkv = np.ascontiguousarray(np.asarray(inp["L0_attn_a_W_kv_down"], f32))
    nc = build_l1()
    maps = []
    for j in range(NCORES):
        maps.append(dict(x=np.ascontiguousarray(xT[:, :, j * TS:(j + 1) * TS]),
                         wkv=wkv, onesb=onesb, ones1=ones1, ones8=ones8))
    r1 = _run(nc, maps).results
    xln1f = np.concatenate([r["xln1"] for r in r1], axis=2)
    ckvf = np.concatenate([r["ckv"] for r in r1], axis=2)

    wq = np.asarray(inp["L0_attn_a_W_q"], f32)
    wqr_full = np.asarray(inp["L0_attn_a_W_q_rope"], f32)
    wku = np.asarray(inp["L0_attn_a_W_k_up"], f32)
    wkr_full = np.asarray(inp["L0_attn_a_W_k_rope"], f32)
    wv = np.asarray(inp["L0_attn_a_W_v_up"], f32)
    half = DR // 2
    wqr_a = np.zeros((D, 64), f32); wqr_a[:, 0:half] = wqr_full[:, 0:half]; wqr_a[:, 32:32 + half] = wqr_full[:, half:DR]
    wqr_b = np.zeros((D, 64), f32); wqr_b[:, 0:half] = wqr_full[:, half:DR]; wqr_b[:, 32:32 + half] = wqr_full[:, 0:half]
    wkr_a = np.zeros((DC, 64), f32); wkr_a[:, 0:half] = wkr_full[:, 0:half]; wkr_a[:, 32:32 + half] = wkr_full[:, half:DR]
    wkr_b = np.zeros((DC, 64), f32); wkr_b[:, 0:half] = wkr_full[:, half:DR]; wkr_b[:, 32:32 + half] = wkr_full[:, 0:half]
    inv_freq = (1.0 / 10000.0 ** (np.arange(0, DR, 2, dtype=f32) / DR)).astype(f32)
    tt = np.arange(T, dtype=f32)
    freqs = np.outer(tt, inv_freq)
    cosT = np.cos(freqs).T.astype(f32)
    sinT = np.sin(freqs).T.astype(f32)
    cosrep = np.zeros((128, T), f32)
    sinsgn = np.zeros((128, T), f32)
    for base, sgn in ((0, -1.0), (32, 1.0), (64, -1.0), (96, 1.0)):
        cosrep[base:base + half] = cosT
        sinsgn[base:base + half] = sgn * sinT
    masks = np.zeros((4, 128, QC), f32)
    kk_idx = np.arange(128)[:, None]
    qq_idx = np.arange(QC)[None, :]
    for i in range(4):
        masks[i] = (128 * i + kk_idx <= qq_idx).astype(f32)
    ident = np.eye(128, dtype=f32)
    csts_arr = np.zeros((128, 2), f32)
    csts_arr[:, 0] = QSCALE
    csts_arr[:, 1] = ESHIFT

    nc = build_l2()
    maps = []
    for j in range(NCORES):
        hsl = slice(j * 128, (j + 1) * 128)
        wva = np.zeros((DC, 128), f32)
        wva[:, 0:64] = wv[:, (2 * j) * 64:(2 * j + 1) * 64]
        wva[:, 64:128] = wv[:, (2 * j + 1) * 64:(2 * j + 2) * 64]
        maps.append(dict(xlnf=xln1f, ckvf=ckvf,
                         wqh=np.ascontiguousarray(wq[:, hsl]),
                         wqr=wqr_a, wqr2=wqr_b,
                         wkuh=np.ascontiguousarray(wku[:, hsl]),
                         wkr=wkr_a, wkr2=wkr_b, wvarr=wva,
                         cosrep=cosrep, sinsgn=sinsgn, masks=masks, ident=ident,
                         csts=csts_arr))
    r2 = _run(nc, maps).results
    attnf = np.concatenate([r["attn"] for r in r2], axis=1)

    wo = np.ascontiguousarray(np.asarray(inp["L0_attn_a_W_o"], f32))
    wr = np.asarray(inp["L0_ffn_Wr"], f32)
    perm = [7, 0, 1, 2, 3, 4, 5, 6]
    wrt = np.ascontiguousarray(wr[perm].T)
    w1 = np.ascontiguousarray(np.asarray(inp["L0_ffn_W1"], f32)[7])
    b1 = np.ascontiguousarray(np.asarray(inp["L0_ffn_b1"], f32)[7].reshape(NTF, 128).T)
    w2 = np.ascontiguousarray(np.asarray(inp["L0_ffn_W2"], f32)[7])
    b2 = np.ascontiguousarray(np.asarray(inp["L0_ffn_b2"], f32)[7].reshape(NTD, 128).T)
    gamma = np.ascontiguousarray(np.asarray(inp["L0_attn_b_gamma"], f32).reshape(NTD, 128).T)
    beta = np.ascontiguousarray(np.asarray(inp["L0_attn_b_beta"], f32).reshape(NTD, 128).T)

    nc = build_l3()
    maps = []
    for j in range(NCORES):
        sl = slice(j * TS, (j + 1) * TS)
        maps.append(dict(attn=np.ascontiguousarray(attnf[:, :, sl]),
                         x=np.ascontiguousarray(xT[:, :, sl]),
                         wo=wo, wrt=wrt, w1=w1, b1=b1, w2=w2, b2=b2,
                         gamma=gamma, beta=beta,
                         onesb=onesb, ones1=ones1, ones8=ones8))
    r3 = _run(nc, maps).results
    x2f = np.concatenate([r["x2"] for r in r3], axis=2)
    xl3af = np.concatenate([r["xl3a"] for r in r3], axis=2)
    xln3f = np.concatenate([r["xln3"] for r in r3], axis=2)

    w_in = np.asarray(inp["L0_attn_b_W_in"], f32)
    w_conv = np.asarray(inp["L0_attn_b_W_conv"], f32)
    b_conv = np.asarray(inp["L0_attn_b_b_conv"], f32)
    w_x = np.asarray(inp["L0_attn_b_W_x"], f32)
    wxa_full = np.zeros((DI, 128), f32)
    wxa_full[:, 0:NST] = w_x[:, DTR:DTR + NST]
    wxa_full[:, 32:32 + NST] = w_x[:, DTR + NST:DTR + 2 * NST]
    wxa_full[:, 64:64 + DTR] = w_x[:, 0:DTR]

    nc = build_l4()
    maps = []
    for j in range(NCORES):
        dsl = slice(j * DSL, (j + 1) * DSL)
        cvd = np.zeros((2, 4, 128, 128), f32)
        for tl in range(2):
            for tap in range(4):
                dvals = w_conv[j * DSL + tl * 128: j * DSL + (tl + 1) * 128, tap]
                cvd[tl, tap][np.arange(128), np.arange(128)] = dvals
        bcv = np.ascontiguousarray(b_conv[dsl].reshape(2, 128).T)
        maps.append(dict(xln3f=xln3f,
                         winx=np.ascontiguousarray(w_in[:, dsl]),
                         winz=np.ascontiguousarray(w_in[:, DI + j * DSL: DI + (j + 1) * DSL]),
                         cvd=cvd, bcv=bcv,
                         wxa=np.ascontiguousarray(wxa_full[dsl])))
    r4 = _run(nc, maps).results
    uf = [r["u"] for r in r4]
    zf_slices = [r["z"] for r in r4]
    xparts = np.stack([r["xdblp"] for r in r4], axis=0)

    a_log = np.asarray(inp["L0_attn_b_A_log"], f32)
    w_dt = np.asarray(inp["L0_attn_b_W_dt"], f32)
    b_dt = np.asarray(inp["L0_attn_b_b_dt"], f32)
    dskip = np.asarray(inp["L0_attn_b_Dskip"], f32)

    nc = build_l5()
    maps = []
    for j in range(NCORES):
        dsl = slice(j * DSL, (j + 1) * DSL)
        alog_arr = np.zeros((128, 32), f32)
        for tl in range(2):
            alog_arr[:, tl * 16:(tl + 1) * 16] = a_log[j * DSL + tl * 128: j * DSL + (tl + 1) * 128]
        wdt_t = np.stack([np.ascontiguousarray(w_dt[:, j * DSL + tl * 128: j * DSL + (tl + 1) * 128])
                          for tl in range(2)], axis=0)
        maps.append(dict(xparts=xparts, u=uf[j], alog=alog_arr, wdt=wdt_t,
                         bdt=np.ascontiguousarray(b_dt[dsl].reshape(2, 128).T),
                         dsk=np.ascontiguousarray(dskip[dsl].reshape(2, 128).T),
                         ones1=ones1))
    r5 = _run(nc, maps).results
    y_full = np.concatenate([r["y"] for r in r5], axis=1)
    z_full = np.concatenate(zf_slices, axis=1)

    wout = np.ascontiguousarray(np.asarray(inp["L0_attn_b_W_out"], f32))
    nc = build_l6()
    maps = []
    for j in range(NCORES):
        sl = slice(j * TS, (j + 1) * TS)
        maps.append(dict(yf=np.ascontiguousarray(y_full[:, :, sl]),
                         zf=np.ascontiguousarray(z_full[:, :, sl]),
                         x2=np.ascontiguousarray(x2f[:, :, sl]),
                         xl3a=np.ascontiguousarray(xl3af[:, :, sl]),
                         wout=wout, wrt=wrt, w1=w1, b1=b1, w2=w2, b2=b2,
                         onesb=onesb, ones1=ones1, ones8=ones8))
    r6 = _run(nc, maps).results
    outT = np.concatenate([r["o"] for r in r6], axis=2)
    return np.ascontiguousarray(np.transpose(outT, (0, 2, 1)))
